# version 20
# speedup vs baseline: 1.0954x; 1.0142x over previous
import contextlib
import numpy as np
import concourse.bass as bass
import concourse.mybir as mybir
from concourse.bass_utils import run_bass_kernel_spmd

F32 = mybir.dt.float32
BF16 = mybir.dt.bfloat16
U32 = mybir.dt.uint32
AF = mybir.ActivationFunctionType
ALU = mybir.AluOpType
AX = mybir.AxisListType

T = 4096
D = 1024
I_FFN = 2816
NT = T // 128
KD = D // 128
KI = I_FFN // 128

EPS_LN = 1e-5
EPS_RMS = 1e-6
F_ONE_BITS = int(np.float32(1.0).view(np.int32))
MHAT = 30.0


ENGINES = ("tensor", "vector", "scalar", "gpsimd", "sync")
DMA_ROT = 6


class _Op:
    __slots__ = ("engine", "fn", "deps", "sdeps", "is_dma", "dma_idx",
                 "needs_inc", "cum", "tag", "relax")

    def __init__(self, engine, fn, is_dma, tag):
        self.relax = False
        self.engine = engine
        self.fn = fn
        self.deps = set()
        self.sdeps = set()
        self.is_dma = is_dma
        self.dma_idx = -1
        self.needs_inc = False
        self.cum = 0
        self.tag = tag


def _key(x):
    if isinstance(x, str):
        return x
    return x.name


class Prog:
    def __init__(self, nc, serialize=False):
        self.serialize = serialize
        self.nc = nc
        self.ops = []
        self.bufs = {}
        self.barrier_deps = []
        self.dma_count = {e: 0 for e in ENGINES}

    def _buf(self, key):
        b = self.bufs.get(key)
        if b is None:
            b = self.bufs[key] = {"w": {}, "r": {}}
        return b

    def add(self, engine, fn, reads=(), writes=(), sreads=(), dma=False,
            tag="", relax=False):
        op = _Op(engine, fn, dma, tag)
        op.relax = bool(relax) or (engine == "tensor" and not dma)
        if dma:
            op.dma_idx = self.dma_count[engine]
            self.dma_count[engine] += 1
        src = (engine, "dma", op.dma_idx) if dma else engine
        for k in sreads:
            b = self._buf(_key(k))
            op.deps.update(b["w"].values())
            op.sdeps.update(b["w"].values())
            b["r"][src] = op
        for k in reads:
            b = self._buf(_key(k))
            op.deps.update(b["w"].values())
            b["r"][src] = op
        for k in writes:
            b = self._buf(_key(k))
            op.deps.update(b["w"].values())
            op.deps.update(b["r"].values())
            b["w"] = {src: op}
            b["r"] = {src: op}
        op.deps.update(self.barrier_deps)
        op.deps.discard(op)
        self.ops.append(op)
        return op

    def barrier(self):
        last = {}
        for op in self.ops:
            last[(op.engine, op.is_dma, op.dma_idx % DMA_ROT if op.is_dma
                  else 0)] = op
        self.barrier_deps = list(last.values())

    @staticmethod
    def _edge_needed(d, op):
        if d.is_dma:
            return True
        if d.engine != op.engine:
            return True
        if d in op.sdeps:
            return True
        if op.is_dma:
            return True
        return not (d.relax and op.relax)

    def emit(self):
        nc = self.nc
        if self.serialize:
            for i in range(1, len(self.ops)):
                self.ops[i].deps = {self.ops[i - 1]}
        for op in self.ops:
            for d in op.deps:
                if not d.is_dma and self._edge_needed(d, op):
                    d.needs_inc = True
        cum = {e: 0 for e in ENGINES}
        for op in self.ops:
            if not op.is_dma:
                if op.needs_inc:
                    cum[op.engine] += 1
                op.cum = cum[op.engine]
        totals = dict(cum)
        dma_tot = dict(self.dma_count)

        eng_sem = {e: nc.alloc_semaphore(f"pr_{e}") for e in ENGINES}
        dma_sem = {e: [nc.alloc_semaphore(f"dq_{e}_{i}") for i in range(DMA_ROT)]
                   for e in ENGINES if self.dma_count[e] > 0}

        ops_by_engine = {e: [op for op in self.ops if op.engine == e]
                         for e in ENGINES}

        def section_for(engine):
            def section(eng):
                water = {}
                for op in ops_by_engine[engine]:
                    waits = {}
                    for d in op.deps:
                        if d.is_dma:
                            s = dma_sem[d.engine][d.dma_idx % DMA_ROT]
                            v = 16 * (d.dma_idx // DMA_ROT + 1)
                        else:
                            if not self._edge_needed(d, op):
                                continue
                            s = eng_sem[d.engine]
                            v = d.cum
                        if v > waits.get(s, 0):
                            waits[s] = v
                    if op.is_dma and op.dma_idx >= DMA_ROT:
                        s = dma_sem[engine][op.dma_idx % DMA_ROT]
                        v = 16 * (op.dma_idx // DMA_ROT)
                        if v > waits.get(s, 0):
                            waits[s] = v
                    for s, v in waits.items():
                        if water.get(s.name, 0) < v:
                            eng.wait_ge(s, v)
                            water[s.name] = v
                    inst = op.fn(eng)
                    if op.is_dma:
                        inst.then_inc(dma_sem[engine][op.dma_idx % DMA_ROT], 16)
                    elif op.needs_inc:
                        inst.then_inc(eng_sem[engine], 1)
            return section

        with nc.Block() as block:
            block.sync(section_for("sync"))
            block.scalar(section_for("scalar"))
            block.vector(section_for("vector"))
            block.tensor(section_for("tensor"))
            block.gpsimd(section_for("gpsimd"))

            def finale(eng):
                for e in ENGINES:
                    if totals[e] > 0:
                        eng.wait_ge(eng_sem[e], totals[e])
                for e, n in dma_tot.items():
                    if n <= 0:
                        continue
                    for i in range(DMA_ROT):
                        uses = (n - i + DMA_ROT - 1) // DMA_ROT
                        if uses > 0:
                            eng.wait_ge(dma_sem[e][i], 16 * uses)
            block.sync(finale)
        return nc



class QuantScratch:

    def __init__(self, nc, name, w, A=None):
        self.w = w
        nb = w // 16
        A = A or nc.alloc_sbuf_tensor
        self.bm = A(f"{name}_bm", [128, nb], F32)
        self.r6 = A(f"{name}_r6", [128, nb], F32)
        self.s6 = A(f"{name}_s6", [128, nb], F32)
        self.xn = A(f"{name}_xn", [128, w], F32)
        self.wa = A(f"{name}_wa", [128, w], F32)
        self.b1 = A(f"{name}_b1", [128, w], F32)
        self.b2 = A(f"{name}_b2", [128, w], F32)


def emit_quant(p, sc, ucsb, src_ap, src_keys, out_ap, out_keys, w=None):
    w = w or sc.w
    nb = w // 16
    s = lambda t: t[:, 0:w]
    sb = lambda t: t[:, 0:nb]
    p.add("vector", lambda e: e.reduce_max(
        sb(sc.bm), src_ap.rearrange("p (b i) -> p b i", i=16), axis=AX.X,
        apply_absolute_value=True), reads=src_keys, writes=[sc.bm])
    p.add("vector", lambda e: e.reciprocal(sb(sc.r6), sb(sc.bm)),
          reads=[sc.bm], writes=[sc.r6])
    p.add("vector", lambda e: e.tensor_scalar(
        sb(sc.r6), sb(sc.r6), 6.0, 1e12, ALU.mult, ALU.min),
        reads=[sc.r6], writes=[sc.r6])
    p.add("vector", lambda e: e.tensor_scalar(
        sb(sc.s6), sb(sc.bm), float(np.float32(1.0 / 6.0)), 1e-12,
        ALU.mult, ALU.max), reads=[sc.bm], writes=[sc.s6])
    p.add("vector", lambda e: e.tensor_tensor(
        s(sc.xn), src_ap.rearrange("p (b i) -> p b i", i=16),
        bass.AP(sc.r6, 0, [[sc.w // 16, 128], [1, nb], [0, 16]]), ALU.mult),
        reads=list(src_keys) + [sc.r6], writes=[sc.xn], relax=True)
    p.add("vector", lambda e: e.tensor_scalar(
        s(sc.wa).bitcast(U32), s(sc.xn).bitcast(U32), 0x7FFFFFFF, None,
        ALU.bitwise_and), reads=[sc.xn], writes=[sc.wa], relax=True)
    p.add("vector", lambda e: e.tensor_scalar(
        s(sc.b1).bitcast(U32), s(sc.wa).bitcast(U32), F_ONE_BITS,
        0x00200000, ALU.max, ALU.add), reads=[sc.wa], writes=[sc.b1],
        relax=True)
    p.add("vector", lambda e: e.tensor_scalar(
        s(sc.b1).bitcast(U32), s(sc.b1).bitcast(U32), 22, 22,
        ALU.logical_shift_right, ALU.logical_shift_left),
        reads=[sc.b1], writes=[sc.b1], relax=True)
    p.add("vector", lambda e: e.scalar_tensor_tensor(
        s(sc.b1), s(sc.wa), 0.75, s(sc.b1), ALU.is_gt, ALU.mult),
        reads=[sc.wa, sc.b1], writes=[sc.b1], relax=True)
    p.add("vector", lambda e: e.tensor_scalar(
        s(sc.b2), s(sc.wa), 0.25, 0.5, ALU.is_gt, ALU.mult),
        reads=[sc.wa], writes=[sc.b2], relax=True)
    p.add("vector", lambda e: e.tensor_tensor(
        s(sc.b1), s(sc.b1), s(sc.b2), ALU.max),
        reads=[sc.b1, sc.b2], writes=[sc.b1], relax=True)
    p.add("vector", lambda e: e.scalar_tensor_tensor(
        s(sc.wa).bitcast(U32), s(sc.xn).bitcast(U32), ucsb[:, 0:1],
        s(sc.b1).bitcast(U32), ALU.bitwise_and, ALU.bitwise_or),
        reads=[sc.xn, sc.b1, "uconst"], writes=[sc.wa], relax=True)
    p.add("vector", lambda e: e.tensor_tensor(
        out_ap, s(sc.wa).rearrange("p (b i) -> p b i", i=16),
        bass.AP(sc.s6, 0, [[sc.w // 16, 128], [1, nb], [0, 16]]), ALU.mult),
        reads=[sc.wa, sc.s6], writes=out_keys, relax=True)


RSQRT_MAGIC1 = 0x5F3759E0


def emit_rsqrt(p, v, out, scr, nrows=128):
    r = lambda t: t[0:nrows, :]
    p.add("vector", lambda e: e.tensor_scalar(
        r(out).bitcast(U32), r(v).bitcast(U32), 1, None,
        ALU.logical_shift_right), reads=[v], writes=[out])
    p.add("vector", lambda e: e.tensor_scalar(
        r(out).bitcast(U32), r(out).bitcast(U32), 0x7FFFFFFF, None,
        ALU.bitwise_xor), reads=[out], writes=[out])
    p.add("vector", lambda e: e.tensor_scalar(
        r(out).bitcast(U32), r(out).bitcast(U32), 0, RSQRT_MAGIC1,
        ALU.max, ALU.add), reads=[out], writes=[out])
    p.add("vector", lambda e: e.tensor_scalar(
        r(out).bitcast(U32), r(out).bitcast(U32), 0x7FFFFFFF, None,
        ALU.bitwise_and), reads=[out], writes=[out])
    for _ in range(2):
        p.add("vector", lambda e: e.tensor_tensor(
            r(scr), r(out), r(out), ALU.mult), reads=[out], writes=[scr])
        p.add("vector", lambda e: e.tensor_tensor(
            r(scr), r(scr), r(v), ALU.mult), reads=[scr, v], writes=[scr])
        p.add("vector", lambda e: e.tensor_scalar(
            r(scr), r(scr), -0.5, 1.5, ALU.mult, ALU.add),
            reads=[scr], writes=[scr])
        p.add("vector", lambda e: e.tensor_tensor(
            r(out), r(out), r(scr), ALU.mult), reads=[out, scr],
            writes=[out])


def emit_ln(p, nc, st, x_key, xt, xn_ap, xn_keys, nrows=128, w=D):
    r = lambda t: t[0:nrows, :]
    p.add("scalar", lambda e: e.activation(st["sq"][0:nrows, 0:w], xt,
          AF.Square, accum_out=r(st["s2"])), reads=[x_key],
          writes=[st["sq"], st["s2"]])
    p.add("scalar", lambda e: e.activation(st["sq"][0:nrows, 0:w], xt,
          AF.Copy, accum_out=r(st["s1"])), reads=[x_key],
          writes=[st["sq"], st["s1"]])
    inv = 1.0 / w
    p.add("vector", lambda e: e.tensor_scalar(
        r(st["mean"]), r(st["s1"]), inv, None, ALU.mult),
        reads=[st["s1"]], writes=[st["mean"]])
    p.add("vector", lambda e: e.tensor_scalar(
        r(st["s2"]), r(st["s2"]), inv, None, ALU.mult),
        reads=[st["s2"]], writes=[st["s2"]])
    p.add("vector", lambda e: e.tensor_scalar(
        r(st["nsq"]), r(st["mean"]), st["mean"][0:nrows, :], -1.0,
        ALU.mult, ALU.mult), reads=[st["mean"]], sreads=[st["mean"]],
        writes=[st["nsq"]])
    p.add("vector", lambda e: e.tensor_scalar(
        r(st["std"]), r(st["s2"]), st["nsq"][0:nrows, :], EPS_LN,
        ALU.add, ALU.add), reads=[st["s2"]], sreads=[st["nsq"]],
        writes=[st["std"]])
    p.add("scalar", lambda e: e.activation(
        r(st["std"]), r(st["std"]), AF.Sqrt),
        reads=[st["std"]], writes=[st["std"]])
    p.add("vector", lambda e: e.reciprocal(r(st["rstd"]), r(st["std"])),
          reads=[st["std"]], writes=[st["rstd"]])
    p.add("vector", lambda e: e.tensor_scalar(
        r(st["nb"]), r(st["mean"]), st["rstd"][0:nrows, :], -1.0,
        ALU.mult, ALU.mult), reads=[st["mean"]], sreads=[st["rstd"]],
        writes=[st["nb"]])
    p.add("vector", lambda e: e.tensor_scalar(
        xn_ap, xt, st["rstd"][0:nrows, :], st["nb"][0:nrows, :],
        ALU.mult, ALU.add),
        reads=[x_key], sreads=[st["nb"], st["rstd"]], writes=xn_keys)


def ln_scratch(nc, name, w=D, A=None, sq=None):
    A = A or nc.alloc_sbuf_tensor
    return {"sq": sq if sq is not None else A(f"{name}_sq", [128, w], F32),
            "s1": A(f"{name}_s1", [128, 1], F32),
            "s2": A(f"{name}_s2", [128, 1], F32),
            "mean": A(f"{name}_mean", [128, 1], F32),
            "std": A(f"{name}_std", [128, 1], F32),
            "rstd": A(f"{name}_rstd", [128, 1], F32),
            "nb": A(f"{name}_nb", [128, 1], F32),
            "nsq": A(f"{name}_nsq", [128, 1], F32)}



def build_phaseF(nc, p, dram, ln_cache=None):
    x0 = dram["x0"]
    wg = dram["wg"]
    wu = dram["wu"]
    wd = dram["wd"]
    g0 = dram["g0"]
    wdT = dram["wdT"]
    es = contextlib.ExitStack()
    A = lambda n, sh, dt: es.enter_context(nc.sbuf_tensor(n, sh, dt))
    P = lambda n, sh, dt=F32: es.enter_context(nc.psum_tensor(n, sh, dt))

    ucsb = A("f_uc", [128, 2], U32)
    p.add("sync", lambda e: e.dma_start(out=ucsb[:], in_=dram["uconst"]),
          dma=True, writes=["uconst"])
    idsb = A("f_id", [128, 128], BF16)
    p.add("sync", lambda e: e.dma_start(out=idsb[:], in_=dram["identb"]),
          dma=True, writes=["identb"])

    es0 = contextlib.ExitStack()
    A0 = lambda n, sh, dt: es0.enter_context(
        nc.sbuf_tensor(n, sh, dt, side="right"))
    qs = QuantScratch(nc, "f_q", I_FFN, A=A0)
    wdld = [es0.enter_context(nc.sbuf_tensor(f"f_wdld{i}", [128, I_FFN], F32, side="right"))
            for i in range(2)]
    es_ps = contextlib.ExitStack()
    tp_ps = [es_ps.enter_context(nc.psum_tensor(f"f_tp{i}", [128, 128], BF16))
             for i in range(2)]

    esA = contextlib.ExitStack()
    wdq = [esA.enter_context(nc.sbuf_tensor(f"f_wdq{t}", [128, I_FFN], BF16, side="right"))
           for t in range(KD)]
    stg = [esA.enter_context(nc.sbuf_tensor(f"f_stg{i}", [128, D], BF16, side="right"))
           for i in range(2)]
    for t in range(KD):
        buf = wdld[t % 2]
        p.add("sync", lambda e, t=t, buf=buf: e.dma_start(
            out=buf[:], in_=wd[128 * t:128 * (t + 1), :]), dma=True,
            writes=[buf])
        emit_quant(p, qs, ucsb, buf[:], [buf], wdq[t][:], [wdq[t]])
    for c in range(KI):
        sg = stg[c % 2]
        for t in range(KD):
            ps = tp_ps[t % 2]
            p.add("tensor", lambda e, c=c, t=t, ps=ps: e.transpose(
                ps[:], wdq[t][:, 128 * c:128 * (c + 1)], idsb[:]),
                reads=[wdq[t], "identb"], writes=[ps])
            p.add("scalar", lambda e, t=t, ps=ps, sg=sg: e.copy(
                sg[:, 128 * t:128 * (t + 1)], ps[:]),
                reads=[ps], writes=[sg])
        p.add("scalar", lambda e, c=c, sg=sg: e.dma_start(
            out=wdT[c], in_=sg[:]), dma=True, reads=[sg], writes=["wdT"])
    p.barrier()
    esA.close()

    wgqT = [A(f"f_wgqT{k}", [128, I_FFN], BF16) for k in range(KD)]
    wupqT = [A(f"f_wupqT{k}", [128, I_FFN], BF16) for k in range(KD)]
    esB = contextlib.ExitStack()
    wrowq = [esB.enter_context(nc.sbuf_tensor(f"f_wrowq{i}", [128, D], BF16, side="right"))
             for i in range(2)]
    for wsrc, wdstT in ((wg, wgqT), (wu, wupqT)):
        for r in range(KI):
            buf = wdld[r % 2]
            p.add("sync", lambda e, r=r, buf=buf, wsrc=wsrc: e.dma_start(
                out=buf[:, 0:D], in_=wsrc[128 * r:128 * (r + 1), :]),
                dma=True, writes=[buf])
            qout = wrowq[r % 2]
            emit_quant(p, qs, ucsb, buf[:, 0:D], [buf], qout[:], [qout], w=D)
            for k in range(KD):
                ps = tp_ps[k % 2]
                p.add("tensor", lambda e, k=k, qout=qout, ps=ps: e.transpose(
                    ps[:], qout[:, 128 * k:128 * (k + 1)], idsb[:]),
                    reads=[qout, "identb"], writes=[ps])
                dst = wdstT[k]
                p.add("scalar", lambda e, r=r, ps=ps, dst=dst: e.copy(
                    dst[:, 128 * r:128 * (r + 1)], ps[:]),
                    reads=[ps], writes=[dst])
    p.barrier()
    esB.close()
    es0.close()
    es_ps.close()

    qs_x = QuantScratch(nc, "f_qx", D, A=A)
    qs_h = QuantScratch(nc, "f_qh", I_FFN // 4, A=A)
    lnst = ln_scratch(nc, "f_ln", A=A, sq=qs_x.wa)
    xld_ = A("f_xld0", [128, D], F32)
    xld = [xld_, xld_]
    xqb_ = A("f_xq0", [128, D], BF16)
    xqb = [xqb_, xqb_]
    xqTa = A("f_xqTa", [128, KD * 512], BF16)
    xqT = [xqTa[:, 512 * k:512 * (k + 1)] for k in range(KD)]
    gp_ps2 = [P(f"f_gp{i}", [128, 512]) for i in range(2)]
    up_ps2 = [P(f"f_up{i}", [128, 512]) for i in range(2)]
    h_sb = [A(f"f_h{i}", [128, I_FFN], F32) for i in range(2)]
    hq_ = A("f_hq0", [128, I_FFN], BF16)
    hq_sb = [hq_, hq_]
    hqTa = A("f_hqTa", [128, KI * 512], BF16)
    hqT = [hqTa[:, 512 * k:512 * (k + 1)] for k in range(KI)]
    wdbuf = [A(f"f_wdb{i}", [128, D], BF16) for i in range(4)]
    dn_ps = [P(f"f_dn{i}", [128, D]) for i in range(2)]
    gout = A("f_gout", [128, D], F32)

    ICH = [(c * 512, min(512, I_FFN - c * 512))
           for c in range((I_FFN + 511) // 512)]

    def emit_hquant_quarter(gt, hf):
        hs = h_sb[gt % 2]
        hq = hq_sb[gt % 2]
        w2 = I_FFN // 4
        emit_quant(p, qs_h, ucsb, hs[:, w2 * hf:w2 * (hf + 1)], [hs],
                   hq[:, w2 * hf:w2 * (hf + 1)], [hq], w=w2)

    def emit_hquant_tp(gt):
        hq = hq_sb[gt % 2]
        tt = gt % 4
        p.add("scalar", lambda e, tt=tt, hq=hq: e.dma_start_transpose(
            hqTa[:].rearrange("p (c q) -> p c q", q=512)[:, :,
                              128 * tt:128 * (tt + 1)],
            hq[:].rearrange("p (c q) -> p c q", q=128)), dma=True,
            reads=[hq], writes=[f"f_hqT_t{tt}"])

    def emit_hquant(gt):
        for hf in range(4):
            emit_hquant_quarter(gt, hf)
        emit_hquant_tp(gt)

    def emit_down(sp):
        for th in range(2):
            for c in range(KI):
                wb = wdbuf[c % 4]
                p.add("gpsimd", lambda e, c=c, wb=wb: e.dma_start(
                    out=wb[:], in_=wdT[c]), dma=True, reads=["wdT"],
                    writes=[wb])
                for j in range(2):
                    tt = 2 * th + j
                    ps = dn_ps[j]
                    for nh in range(2):
                        p.add("tensor", lambda e, c=c, tt=tt, nh=nh, wb=wb,
                              ps=ps: e.matmul(
                            ps[:, 512 * nh:512 * (nh + 1)],
                            hqT[c][:, 128 * tt:128 * (tt + 1)],
                            wb[:, 512 * nh:512 * (nh + 1)],
                            start=(c == 0), stop=(c == KI - 1)),
                            reads=[f"f_hqT_t{tt}", wb], writes=[ps])
            for j in range(2):
                tt = 2 * th + j
                gt = sp * 4 + tt
                ps = dn_ps[j]
                p.add("scalar", lambda e, ps=ps: e.activation(
                    gout[:], ps[:], AF.Silu), reads=[ps], writes=["f_gout"])
                p.add("scalar", lambda e, gt=gt: e.dma_start(
                    out=g0[128 * gt:128 * (gt + 1), :], in_=gout[:]),
                    dma=True, reads=["f_gout"], writes=["g0"])

    for gt in range(NT):
        tt = gt % 4
        xb = xld[gt % 2]
        p.add("sync", lambda e, gt=gt, xb=xb: e.dma_start(
            out=xb[:], in_=x0[128 * gt:128 * (gt + 1), :]), dma=True,
            writes=[xb])
        emit_ln(p, nc, lnst, xb, xb[:], qs_x.xn[:], [qs_x.xn])
        if ln_cache is not None:
            p.add("vector", lambda e, gt=gt: e.tensor_copy(
                ln_cache[0][:, gt:gt + 1], lnst["rstd"][:]),
                reads=[lnst["rstd"]], writes=["ln0_rstd"])
            p.add("vector", lambda e, gt=gt: e.tensor_copy(
                ln_cache[1][:, gt:gt + 1], lnst["nb"][:]),
                reads=[lnst["nb"]], writes=["ln0_nb"])
        xq = xqb[gt % 2]
        emit_quant(p, qs_x, ucsb, qs_x.xn[:], [qs_x.xn], xq[:], [xq], w=D)
        p.add("sync", lambda e, tt=tt, xq=xq: e.dma_start_transpose(
            xqTa[:].rearrange("p (c q) -> p c q", q=512)[:, :,
                              128 * tt:128 * (tt + 1)],
            xq[:].rearrange("p (c q) -> p c q", q=128)), dma=True,
            reads=[xq], writes=[f"f_xqT_t{tt}"])
        hs = h_sb[gt % 2]
        for ci, (i0, iw) in enumerate(ICH):
            if gt > 0 and ci < 4:
                emit_hquant_quarter(gt - 1, ci)
            gp_ps = gp_ps2[ci % 2]
            up_ps = up_ps2[ci % 2]
            for k in range(KD):
                p.add("tensor", lambda e, k=k, tt=tt, i0=i0, iw=iw,
                      gp_ps=gp_ps: e.matmul(
                    gp_ps[:, 0:iw], xqT[k][:, 128 * tt:128 * (tt + 1)],
                    wgqT[k][:, i0:i0 + iw], start=(k == 0),
                    stop=(k == KD - 1)),
                    reads=[f"f_xqT_t{tt}", wgqT[k]], writes=[gp_ps])
            p.add("scalar", lambda e, i0=i0, iw=iw, hs=hs, gp_ps=gp_ps:
                  e.activation(
                hs[:, i0:i0 + iw], gp_ps[:, 0:iw], AF.Silu),
                reads=[gp_ps], writes=[hs])
            for k in range(KD):
                p.add("tensor", lambda e, k=k, tt=tt, i0=i0, iw=iw,
                      up_ps=up_ps: e.matmul(
                    up_ps[:, 0:iw], xqT[k][:, 128 * tt:128 * (tt + 1)],
                    wupqT[k][:, i0:i0 + iw], start=(k == 0),
                    stop=(k == KD - 1)),
                    reads=[f"f_xqT_t{tt}", wupqT[k]], writes=[up_ps])
            p.add("vector", lambda e, i0=i0, iw=iw, hs=hs, up_ps=up_ps:
                  e.tensor_tensor(
                hs[:, i0:i0 + iw], hs[:, i0:i0 + iw], up_ps[:, 0:iw],
                ALU.mult), reads=[hs, up_ps], writes=[hs])
        if gt > 0:
            emit_hquant_tp(gt - 1)
        if gt % 4 == 0 and gt > 0:
            emit_down(gt // 4 - 1)
    emit_hquant(NT - 1)
    emit_down(T // 512 - 1)
    p.barrier()
    es.close()
    return p



def build_phaseA(nc, p, dram, ln_cache=None):
    x0 = dram["x0"]
    g0 = dram["g0"]
    w_in = dram["w_in"]
    wnorm = dram["wnorm"]
    lam = dram["lam"]
    wout = dram["wout"]
    x1 = dram["x1"]
    es = contextlib.ExitStack()
    A = lambda n, sh, dt: es.enter_context(nc.sbuf_tensor(n, sh, dt))
    P = lambda n, sh, dt=F32: es.enter_context(nc.psum_tensor(n, sh, dt))

    idsb = A("a_idf", [128, 128], F32)
    p.add("sync", lambda e: e.dma_start(out=idsb[:], in_=dram["identf"]),
          dma=True, writes=["identf"])

    lampn = A("a_lampn", [128, 8], F32)
    lampp = A("a_lampp", [128, 8], F32)
    wnp = A("a_wnp", [128, 8], F32)

    for k in range(KD):
        p.add("sync", lambda e, k=k: e.dma_start(
            out=lampp[:, k:k + 1],
            in_=lam[128 * k:128 * (k + 1)].rearrange("(p o) -> p o", o=1)),
            dma=True, writes=["a_lampp"])
        p.add("sync", lambda e, k=k: e.dma_start(
            out=wnp[:, k:k + 1],
            in_=wnorm[128 * k:128 * (k + 1)].rearrange("(p o) -> p o", o=1)),
            dma=True, writes=["a_wnp"])
    p.add("scalar", lambda e: e.activation(lampp[:], lampp[:], AF.Sigmoid),
          reads=["a_lampp"], writes=["a_lampp"])
    p.add("vector", lambda e: e.tensor_scalar(
        lampn[:], lampp[:], -0.5, None, ALU.mult),
        reads=["a_lampp"], writes=["a_lampn"])
    p.add("vector", lambda e: e.tensor_scalar(
        lampp[:], lampp[:], 0.5, None, ALU.mult),
        reads=["a_lampp"], writes=["a_lampp"])

    lnst = ln_scratch(nc, "a_ln", A=A)
    rsq = A("a_rsq", [128, D], F32)
    rss = A("a_rss", [128, 1], F32)
    rr = A("a_rr", [128, 1], F32)
    xld = A("a_xld", [128, D], F32)
    xnb = A("a_xn", [128, D], F32)
    xrb = A("a_xr", [128, D], BF16)
    xrT = [A(f"a_xrT{k}", [128, T], BF16) for k in range(KD)]
    for gt in range(NT):
        p.add("sync", lambda e, gt=gt: e.dma_start(
            out=xld[:], in_=x0[128 * gt:128 * (gt + 1), :]), dma=True,
            writes=["a_xld"])
        if ln_cache is None:
            emit_ln(p, nc, lnst, "a_xld", xld[:], xnb[:], ["a_xn"])
        else:
            p.add("vector", lambda e, gt=gt: e.tensor_scalar(
                xnb[:], xld[:], ln_cache[0][:, gt:gt + 1],
                ln_cache[1][:, gt:gt + 1], ALU.mult, ALU.add),
                reads=["a_xld"], sreads=["ln0_rstd", "ln0_nb"],
                writes=["a_xn"])
        p.add("scalar", lambda e: e.activation(
            rsq[:], xnb[:], AF.Square, accum_out=rss[:]),
            reads=["a_xn"], writes=["a_rsq", "a_rss"])
        p.add("vector", lambda e: e.tensor_scalar(
            rss[:], rss[:], 1.0 / D, EPS_RMS, ALU.mult, ALU.add),
            reads=["a_rss"], writes=["a_rss"])
        p.add("scalar", lambda e: e.activation(rss[:], rss[:], AF.Sqrt),
              reads=["a_rss"], writes=["a_rss"])
        p.add("vector", lambda e: e.reciprocal(rr[:], rss[:]),
              reads=["a_rss"], writes=["a_rr"])
        p.add("vector", lambda e: e.tensor_scalar(
            xrb[:], xnb[:], rr[:], None, ALU.mult),
            reads=["a_xn"], sreads=["a_rr"], writes=["a_xr"])
        for k in range(KD):
            p.add("sync", lambda e, k=k, gt=gt: e.dma_start_transpose(
                xrT[k][:, 128 * gt:128 * (gt + 1)],
                xrb[:, 128 * k:128 * (k + 1)]), dma=True,
                reads=["a_xr"], writes=[xrT[k]])

    esA2 = contextlib.ExitStack()
    yT = [A(f"a_yT{pp}", [128, T], BF16) for pp in range(KD)]
    wch = [esA2.enter_context(nc.sbuf_tensor(
        f"a_wch{i}", [128, 3 * D], BF16, side="right")) for i in range(2)]
    wchf = esA2.enter_context(nc.sbuf_tensor(
        "a_wchf", [128, 3 * D], F32, side="right"))
    sc_a = esA2.enter_context(nc.sbuf_tensor("a_sca", [128, 1024], F32,
                                             side="right"))
    sc_t1 = esA2.enter_context(nc.sbuf_tensor("a_sct1", [128, 1024], F32,
                                              side="right"))
    sc_am = esA2.enter_context(nc.sbuf_tensor("a_scam", [128, 1024], F32,
                                              side="right"))
    sc_u = esA2.enter_context(nc.sbuf_tensor("a_scu", [128, 1024], F32,
                                             side="right"))
    es_ps = contextlib.ExitStack()
    ps_ab = [es_ps.enter_context(nc.psum_tensor(f"a_ps{i}", [128, 1024], F32))
             for i in range(2)]

    NQ = 4
    for pp in range(KD):
        wc = wch[pp % 2]
        for fb in range(3):
            c0 = fb * D + pp * 128
            p.add("sync", lambda e, c0=c0, fb=fb, wc=wc: e.dma_start(
                out=wchf[:, 1024 * fb:1024 * (fb + 1)].rearrange(
                    "p (k c) -> p k c", c=128),
                in_=w_in[:, c0:c0 + 128].rearrange("(k p) c -> p k c",
                                                   p=128)),
                dma=True, reads=[], writes=["a_wchf"], tag=f"winld{pp}_{fb}")
            for k in range(KD):
                p.add("scalar", lambda e, fb=fb, k=k, wc=wc: e.activation(
                    wc[:, 1024 * fb + 128 * k:1024 * fb + 128 * (k + 1)],
                    wchf[:, 1024 * fb + 128 * k:1024 * fb + 128 * (k + 1)],
                    AF.Copy, scale=wnp[:, k:k + 1]),
                    reads=["a_wchf", "a_wnp"], writes=[wc])
        for tq in range(NQ):
            t0 = tq * 1024
            trng = lambda c: slice(t0 + 512 * c, t0 + 512 * (c + 1))
            for k in range(KD):
                for c in range(2):
                    p.add("tensor", lambda e, k=k, c=c, wc=wc, trr=trng(c):
                          e.matmul(
                        ps_ab[0][:, 512 * c:512 * (c + 1)],
                        wc[:, 128 * k:128 * (k + 1)],
                        xrT[k][:, trr], start=(k == 0), stop=(k == KD - 1)),
                        reads=[wc, xrT[k]], writes=["a_ps0"])
            p.add("scalar", lambda e: e.activation(
                sc_a[:], ps_ab[0][:], AF.Sigmoid),
                reads=["a_ps0"], writes=["a_sca"])
            for k in range(KD):
                for c in range(2):
                    p.add("tensor", lambda e, k=k, c=c, wc=wc, trr=trng(c):
                          e.matmul(
                        ps_ab[1][:, 512 * c:512 * (c + 1)],
                        wc[:, D + 128 * k:D + 128 * (k + 1)],
                        xrT[k][:, trr], start=(k == 0), stop=(k == KD - 1)),
                        reads=[wc, xrT[k]], writes=["a_ps1"])
            p.add("scalar", lambda e: e.activation(
                sc_t1[:], ps_ab[1][:], AF.Sigmoid),
                reads=["a_ps1"], writes=["a_sct1"])
            p.add("vector", lambda e, pp=pp: e.tensor_scalar(
                sc_am[:], sc_a[:], lampn[:, pp:pp + 1], lampp[:, pp:pp + 1],
                ALU.mult, ALU.add),
                reads=["a_sca"], sreads=["a_lampn", "a_lampp"],
                writes=["a_scam"])
            for k in range(KD):
                for c in range(2):
                    p.add("tensor", lambda e, k=k, c=c, wc=wc, trr=trng(c):
                          e.matmul(
                        ps_ab[0][:, 512 * c:512 * (c + 1)],
                        wc[:, 2 * D + 128 * k:2 * D + 128 * (k + 1)],
                        xrT[k][:, trr], start=(k == 0), stop=(k == KD - 1)),
                        reads=[wc, xrT[k]], writes=["a_ps0"])
            p.add("scalar", lambda e: e.activation(
                sc_u[:], ps_ab[0][:], AF.Erf, scale=0.7071067811865476),
                reads=["a_ps0"], writes=["a_scu"])
            p.add("vector", lambda e: e.scalar_tensor_tensor(
                sc_u[:], sc_u[:], 1.0, ps_ab[0][:], ALU.add, ALU.mult),
                reads=["a_scu", "a_ps0"], writes=["a_scu"])
            p.add("vector", lambda e: e.tensor_tensor(
                sc_t1[:], sc_t1[:], sc_u[:], ALU.mult),
                reads=["a_sct1", "a_scu"], writes=["a_sct1"])
            p.add("vector", lambda e: e.tensor_tensor(
                sc_u[:], sc_am[:], sc_t1[:], ALU.mult),
                reads=["a_scam", "a_sct1"], writes=["a_scu"])
            yt = yT[pp]
            for c in range(2):
                t0c = t0 + 512 * c
                init = (0.0 if (tq == 0 and c == 0)
                        else yt[:, t0c - 1:t0c])
                p.add("vector", lambda e, c=c, t0c=t0c, init=init, yt=yt:
                      e.tensor_tensor_scan(
                    yt[:, t0c:t0c + 512], sc_a[:, 512 * c:512 * (c + 1)],
                    sc_u[:, 512 * c:512 * (c + 1)], init,
                    ALU.mult, ALU.add),
                    reads=["a_sca", "a_scu"], sreads=[yt], writes=[yt])
    p.barrier()
    esA2.close()
    es_ps.close()

    wob = [A(f"a_wob{k}", [128, D], BF16) for k in range(KD)]
    for k in range(KD):
        p.add("gpsimd", lambda e, k=k: e.dma_start(
            out=wob[k][:], in_=wout[128 * k:128 * (k + 1), :]), dma=True,
            writes=[wob[k]])
    gld = A("a_gld", [128, D], F32)
    sld = A("a_sld", [128, D], F32)
    ops = [P(f"a_ops{i}", [128, D], F32) for i in range(2)]
    for gt in range(NT):
        ps = ops[gt % 2]
        for k in range(KD):
            for c in range(2):
                p.add("tensor", lambda e, k=k, c=c, gt=gt, ps=ps: e.matmul(
                    ps[:, 512 * c:512 * (c + 1)],
                    yT[k][:, 128 * gt:128 * (gt + 1)],
                    wob[k][:, 512 * c:512 * (c + 1)],
                    start=(k == 0), stop=(k == KD - 1)),
                    reads=[yT[k], wob[k]], writes=[ps])
        p.add("sync", lambda e, gt=gt: e.dma_start(
            out=xld[:], in_=x0[128 * gt:128 * (gt + 1), :]), dma=True,
            writes=["a_xld"])
        if ln_cache is None:
            emit_ln(p, nc, lnst, "a_xld", xld[:], xnb[:], ["a_xn"])
        else:
            p.add("vector", lambda e, gt=gt: e.tensor_scalar(
                xnb[:], xld[:], ln_cache[0][:, gt:gt + 1],
                ln_cache[1][:, gt:gt + 1], ALU.mult, ALU.add),
                reads=["a_xld"], sreads=["ln0_rstd", "ln0_nb"],
                writes=["a_xn"])
        p.add("sync", lambda e, gt=gt: e.dma_start(
            out=gld[:], in_=g0[128 * gt:128 * (gt + 1), :]), dma=True,
            writes=["a_gld"])
        p.add("vector", lambda e, ps=ps: e.tensor_tensor(
            sld[:], ps[:], xnb[:], ALU.add), reads=[ps, "a_xn"],
            writes=["a_sld"])
        p.add("vector", lambda e: e.tensor_tensor(
            sld[:], sld[:], gld[:], ALU.mult), reads=["a_sld", "a_gld"],
            writes=["a_sld"])
        p.add("vector", lambda e: e.tensor_tensor(
            sld[:], sld[:], xld[:], ALU.add), reads=["a_sld", "a_xld"],
            writes=["a_sld"])
        p.add("scalar", lambda e, gt=gt: e.dma_start(
            out=x1[128 * gt:128 * (gt + 1), :], in_=sld[:]), dma=True,
            reads=["a_sld"], writes=["x1"])
    p.barrier()
    es.close()
    return p



NH = 16
HD = 64

def build_phaseB(nc, p, dram):
    x1 = dram["x1"]
    wqkv = dram["wqkv"]
    bqkv = dram["bqkv"]
    woutB = dram["woutB"]
    boutB = dram["boutB"]
    s2 = dram["s2"]
    vAll = dram["vAll"]
    es = contextlib.ExitStack()
    A = lambda n, sh, dt: es.enter_context(nc.sbuf_tensor(n, sh, dt))
    P = lambda n, sh, dt=F32: es.enter_context(nc.psum_tensor(n, sh, dt))

    mmask = A("b_mask", [128, 128], F32)
    p.add("sync", lambda e: e.dma_start(out=mmask[:], in_=dram["trimask"]),
          dma=True, writes=["trimask"])
    ones1b = A("b_ones1b", [1, 128], BF16)
    p.add("gpsimd", lambda e: e.memset(ones1b[:], 1.0), writes=["b_ones1b"])
    onesf = A("b_onesf", [1, 64], F32)
    p.add("gpsimd", lambda e: e.memset(onesf[:], 1.0), writes=["b_onesf"])

    bqp = A("b_bqp", [64, NH], F32)
    bkp = A("b_bkp", [64, NH], F32)
    for h in range(NH):
        p.add("sync", lambda e, h=h: e.dma_start(
            out=bqp[:, h:h + 1],
            in_=bqkv[HD * h:HD * (h + 1)].rearrange("(p o) -> p o", o=1)),
            dma=True, writes=["b_bqp"])
        p.add("sync", lambda e, h=h: e.dma_start(
            out=bkp[:, h:h + 1],
            in_=bqkv[D + HD * h:D + HD * (h + 1)].rearrange(
                "(p o) -> p o", o=1)), dma=True, writes=["b_bkp"])
    p.add("vector", lambda e: e.tensor_scalar(
        bqp[:], bqp[:], 0.125, None, ALU.mult), reads=["b_bqp"],
        writes=["b_bqp"])
    bvrow = A("b_bvrow", [1, D], F32)
    bvb = A("b_bvb", [1, D], BF16)
    p.add("sync", lambda e: e.dma_start(
        out=bvrow[:], in_=bqkv[2 * D:3 * D].rearrange("(o d) -> o d", o=1)),
        dma=True, writes=["b_bvrow"])
    p.add("scalar", lambda e: e.activation(bvb[:], bvrow[:], AF.Copy),
          reads=["b_bvrow"], writes=["b_bvb"])
    bobrow = A("b_bobrow", [1, D], F32)
    bob = A("b_bob", [1, D], BF16)
    p.add("sync", lambda e: e.dma_start(
        out=bobrow[:], in_=boutB.rearrange("(o d) -> o d", o=1)),
        dma=True, writes=["b_bobrow"])
    p.add("scalar", lambda e: e.activation(bob[:], bobrow[:], AF.Copy),
          reads=["b_bobrow"], writes=["b_bob"])

    xn1Ta = A("b_xn1Ta", [128, KD * T], BF16)
    xn1T = [xn1Ta[:, T * k:T * (k + 1)] for k in range(KD)]
    esB0 = contextlib.ExitStack()
    lnst = {n: esB0.enter_context(nc.sbuf_tensor(
        f"b_ln_{n}", [128, D if n == "sq" else 1], F32, side="right"))
        for n in ("sq", "s1", "s2", "mean", "std", "rstd", "nb", "nsq")}
    xld = esB0.enter_context(nc.sbuf_tensor("b_xld", [128, D], F32,
                                            side="right"))
    xnb = esB0.enter_context(nc.sbuf_tensor("b_xn", [128, D], F32,
                                            side="right"))
    xnbb = esB0.enter_context(nc.sbuf_tensor("b_xnb", [128, D], BF16,
                                             side="right"))
    wvb = [esB0.enter_context(nc.sbuf_tensor(
        f"b_wvb{k}", [128, D], BF16, side="right")) for k in range(KD)]
    vtile = esB0.enter_context(nc.sbuf_tensor(
        "b_vtile", [128, D], BF16, side="right"))
    es_ps0 = contextlib.ExitStack()
    v_ps = es_ps0.enter_context(nc.psum_tensor("b_vps", [128, D], F32))

    for k in range(KD):
        p.add("gpsimd", lambda e, k=k: e.dma_start(
            out=wvb[k][:], in_=wqkv[128 * k:128 * (k + 1), 2 * D:3 * D]),
            dma=True, writes=[wvb[k]])
    for gt in range(NT):
        p.add("sync", lambda e, gt=gt: e.dma_start(
            out=xld[:], in_=x1[128 * gt:128 * (gt + 1), :]), dma=True,
            writes=["b_xld"])
        emit_ln(p, nc, lnst, "b_xld", xld[:], xnb[:], ["b_xn"])
        p.add("scalar", lambda e: e.activation(xnbb[:], xnb[:], AF.Copy),
              reads=["b_xn"], writes=["b_xnb"])
        p.add("sync", lambda e, gt=gt: e.dma_start_transpose(
            xn1Ta[:].rearrange("p (c q) -> p c q", q=T)[:, :,
                               128 * gt:128 * (gt + 1)],
            xnbb[:].rearrange("p (c q) -> p c q", q=128)), dma=True,
            reads=["b_xnb"], writes=["b_xn1Ta"])
        for c in range(2):
            for k in range(KD):
                p.add("tensor", lambda e, k=k, c=c, gt=gt: e.matmul(
                    v_ps[:, 512 * c:512 * (c + 1)],
                    xn1T[k][:, 128 * gt:128 * (gt + 1)],
                    wvb[k][:, 512 * c:512 * (c + 1)],
                    start=(k == 0), stop=False),
                    reads=[xn1T[k], wvb[k]], writes=["b_vps"])
            p.add("tensor", lambda e, c=c: e.matmul(
                v_ps[:, 512 * c:512 * (c + 1)], ones1b[:],
                bvb[:, 512 * c:512 * (c + 1)], start=False, stop=True),
                reads=["b_ones1b", "b_bvb"], writes=["b_vps"])
        p.add("scalar", lambda e: e.activation(vtile[:], v_ps[:], AF.Copy),
              reads=["b_vps"], writes=["b_vtile"])
        p.add("scalar", lambda e, gt=gt: e.dma_start(
            out=vAll[128 * gt:128 * (gt + 1), :], in_=vtile[:]), dma=True,
            reads=["b_vtile"], writes=["vAll"])
    p.barrier()
    esB0.close()
    es_ps0.close()

    attnT = [A(f"b_attnT{k}", [128, T], BF16) for k in range(KD)]
    qTh = A("b_qTh", [65, T], BF16)
    kTh = A("b_kTh", [65, T], BF16)
    vbuf = A("b_vbuf", [128, NT * 65], BF16)
    wqkb = A("b_wqkb", [128, KD * 128], BF16)
    lrow = A("b_lrow", [1, 512], F32)
    lrec = A("b_lrec", [1, 512], F32)
    sc64 = A("b_sc64", [64, 512], F32)
    esPT = contextlib.ExitStack()
    PT = esPT.enter_context(nc.sbuf_tensor("b_PT", [128, NT * 512], BF16,
                                           side="right"))
    es_ps1 = contextlib.ExitStack()
    st_ps = [es_ps1.enter_context(nc.psum_tensor(f"b_st{i}", [128, 512], F32))
             for i in range(2)]
    qk_ps = [es_ps1.enter_context(nc.psum_tensor(f"b_qk{i}", [128, 512], F32))
             for i in range(2)]
    av_ps = [es_ps1.enter_context(nc.psum_tensor(f"b_av{i}", [65, 512], F32))
             for i in range(2)]
    rb_ps = es_ps1.enter_context(nc.psum_tensor("b_rb", [64, 512], F32))

    p.add("gpsimd", lambda e: e.memset(qTh[64:65, :], -MHAT), writes=["b_qTh"])
    p.add("gpsimd", lambda e: e.memset(kTh[64:65, :], 1.0), writes=["b_kTh"])

    for h in range(NH):
        p.add("gpsimd", lambda e, h=h: e.dma_start(
            out=wqkb[:].rearrange("p (k c) -> p k c", c=128)[:, :, 0:64],
            in_=wqkv[:, HD * h:HD * (h + 1)].rearrange(
                "(k p) c -> p k c", p=128)), dma=True, writes=["b_wqkb"])
        p.add("gpsimd", lambda e, h=h: e.dma_start(
            out=wqkb[:].rearrange("p (k c) -> p k c", c=128)[:, :, 64:128],
            in_=wqkv[:, D + HD * h:D + HD * (h + 1)].rearrange(
                "(k p) c -> p k c", p=128)), dma=True, writes=["b_wqkb"])
        p.add("sync", lambda e, h=h: e.dma_start(
            out=bass.AP(vbuf, 0, [[NT * 65, 128], [65, NT], [1, HD]]),
            in_=vAll[:, HD * h:HD * (h + 1)].rearrange(
                "(kt pp) c -> pp kt c", pp=128)), dma=True,
            reads=["vAll"], writes=["b_vbuf"])
        p.add("gpsimd", lambda e: e.memset(
            bass.AP(vbuf, HD, [[NT * 65, 128], [65, NT], [1, 1]]), 1.0),
            writes=["b_vbuf"])
        for ch in range(8):
            ps = qk_ps[ch % 2]
            for kk in range(KD):
                p.add("tensor", lambda e, kk=kk, ch=ch, ps=ps: e.matmul(
                    ps[:], wqkb[:, 128 * kk:128 * (kk + 1)],
                    xn1T[kk][:, 512 * ch:512 * (ch + 1)],
                    start=(kk == 0), stop=(kk == KD - 1)),
                    reads=["b_wqkb", xn1T[kk]], writes=[f"b_qk{ch % 2}"])
            p.add("vector", lambda e, ch=ch, ps=ps, h=h: e.tensor_scalar(
                qTh[0:64, 512 * ch:512 * (ch + 1)], ps[0:64, :],
                0.125, bqp[:, h:h + 1], ALU.mult, ALU.add),
                reads=[f"b_qk{ch % 2}"], sreads=["b_bqp"], writes=["b_qTh"])
            p.add("vector", lambda e, ch=ch, ps=ps, h=h: e.tensor_scalar(
                kTh[0:64, 512 * ch:512 * (ch + 1)], ps[64:128, :],
                bkp[:, h:h + 1], None, ALU.add),
                reads=[f"b_qk{ch % 2}"], sreads=["b_bkp"], writes=["b_kTh"])
        def emit_normA(g):
            p.add("scalar", lambda e, g=g: e.copy(
                lrow[:], av_ps[g % 2][64:65, :]),
                reads=["b_av%d" % (g % 2)], writes=["b_lrow"])
            p.add("scalar", lambda e, g=g: e.copy(
                sc64[:], av_ps[g % 2][0:64, :]),
                reads=["b_av%d" % (g % 2)], writes=["b_sc64"])
            p.add("vector", lambda e: e.reciprocal(lrec[:], lrow[:]),
                  reads=["b_lrow"], writes=["b_lrec"])

        def emit_normB(h, g):
            q0 = 512 * g
            p.add("tensor", lambda e: e.matmul(
                rb_ps[:], onesf[:], lrec[:], start=True, stop=True),
                reads=["b_onesf", "b_lrec"], writes=["b_rb"])
            dst = attnT[h // 2]
            pb = 64 * (h % 2)
            p.add("vector", lambda e, dst=dst, pb=pb, q0=q0: e.tensor_tensor(
                dst[pb:pb + 64, q0:q0 + 512], sc64[:], rb_ps[:], ALU.mult),
                reads=["b_sc64", "b_rb"], writes=[dst])

        for g in range(8):
            q0 = 512 * g
            kmax = 4 * g + 4
            if g > 0:
                emit_normA(g - 1)
            for kt in range(kmax):
                ps = st_ps[kt % 2]
                m = kt - 4 * g
                if m < 0:
                    p.add("tensor", lambda e, kt=kt, q0=q0, ps=ps: e.matmul(
                        ps[:], kTh[:, 128 * kt:128 * (kt + 1)],
                        qTh[:, q0:q0 + 512], start=True, stop=True),
                        reads=["b_kTh", "b_qTh"], writes=[f"b_st{kt % 2}"])
                else:
                    p.add("tensor", lambda e, kt=kt, q0=q0, m=m, ps=ps:
                          e.matmul(
                        ps[:, 128 * m:512],
                        kTh[:, 128 * kt:128 * (kt + 1)],
                        qTh[:, q0 + 128 * m:q0 + 512], start=True, stop=True),
                        reads=["b_kTh", "b_qTh"], writes=[f"b_st{kt % 2}"])
                    p.add("vector", lambda e, m=m, ps=ps: e.tensor_tensor(
                        ps[:, 128 * m:128 * (m + 1)],
                        ps[:, 128 * m:128 * (m + 1)], mmask[:], ALU.add),
                        reads=[f"b_st{kt % 2}", "trimask"],
                        writes=[f"b_st{kt % 2}"])
                    if m > 0:
                        p.add("gpsimd", lambda e, kt=kt, m=m: e.memset(
                            PT[:, 512 * kt:512 * kt + 128 * m], 0.0),
                            writes=[f"b_PT{kt}"])
                mm = max(m, 0)
                p.add("scalar", lambda e, kt=kt, mm=mm, ps=ps: e.activation(
                    PT[:, 512 * kt + 128 * mm:512 * (kt + 1)],
                    ps[:, 128 * mm:512], AF.Exp),
                    reads=[f"b_st{kt % 2}"], writes=[f"b_PT{kt}"])
                if kt == min(2, kmax - 1) and g > 0:
                    emit_normB(h, g - 1)

            av = av_ps[g % 2]
            for kt in range(kmax):
                p.add("tensor", lambda e, kt=kt, kmax=kmax, av=av: e.matmul(
                    av[:], vbuf[:, 65 * kt:65 * (kt + 1)],
                    PT[:, 512 * kt:512 * (kt + 1)],
                    start=(kt == 0), stop=(kt == kmax - 1)),
                    reads=["b_vbuf", f"b_PT{kt}"], writes=["b_av%d" % (g % 2)])
        emit_normA(7)
        emit_normB(h, 7)
    p.barrier()
    esPT.close()
    es_ps1.close()

    esB2 = contextlib.ExitStack()
    wob2 = [esB2.enter_context(nc.sbuf_tensor(
        f"b_wob{k}", [128, D], BF16, side="right")) for k in range(KD)]
    sout = esB2.enter_context(nc.sbuf_tensor("b_sout", [128, D], F32,
                                             side="right"))
    es_ps2 = contextlib.ExitStack()
    o_ps = [es_ps2.enter_context(nc.psum_tensor(f"b_ops{i}", [128, D], F32))
            for i in range(2)]
    for k in range(KD):
        p.add("gpsimd", lambda e, k=k: e.dma_start(
            out=wob2[k][:], in_=woutB[128 * k:128 * (k + 1), :]), dma=True,
            writes=[wob2[k]])
    for gt in range(NT):
        ps = o_ps[gt % 2]
        for c in range(2):
            for k in range(KD):
                p.add("tensor", lambda e, k=k, c=c, gt=gt, ps=ps: e.matmul(
                    ps[:, 512 * c:512 * (c + 1)],
                    attnT[k][:, 128 * gt:128 * (gt + 1)],
                    wob2[k][:, 512 * c:512 * (c + 1)],
                    start=(k == 0), stop=False),
                    reads=[attnT[k], wob2[k]], writes=[f"b_o{gt % 2}"])
            p.add("tensor", lambda e, c=c, ps=ps: e.matmul(
                ps[:, 512 * c:512 * (c + 1)], ones1b[:],
                bob[:, 512 * c:512 * (c + 1)], start=False, stop=True),
                reads=["b_ones1b", "b_bob"], writes=[f"b_o{gt % 2}"])
        p.add("scalar", lambda e, ps=ps: e.activation(sout[:], ps[:],
              AF.Copy), reads=[f"b_o{gt % 2}"], writes=["b_sout"])
        p.add("scalar", lambda e, gt=gt: e.dma_start(
            out=s2[128 * gt:128 * (gt + 1), :], in_=sout[:]), dma=True,
            reads=["b_sout"], writes=["s2"])
    p.barrier()
    esB2.close()
    es_ps2.close()
    p.barrier()
    es.close()
    return p



NE = 8
HE = 512

def build_phaseC(nc, p, dram, xn1T=None):
    x1 = dram["x1"]
    s2 = dram["s2"]
    rw = dram["router_w"]
    rb = dram["router_b"]
    y = dram["y"]
    es = contextlib.ExitStack()
    A = lambda n, sh, dt: es.enter_context(nc.sbuf_tensor(n, sh, dt))
    P = lambda n, sh, dt=F32: es.enter_context(nc.psum_tensor(n, sh, dt))

    ones1b = A("c_ones1b", [1, 128], BF16)
    p.add("gpsimd", lambda e: e.memset(ones1b[:], 1.0), writes=["c_ones1b"])
    onesf = A("c_onesf", [1, 128], F32)
    p.add("gpsimd", lambda e: e.memset(onesf[:], 1.0), writes=["c_onesf"])
    idf = A("c_idf", [128, 128], F32)
    p.add("sync", lambda e: e.dma_start(out=idf[:], in_=dram["identf"]),
          dma=True, writes=["identf"])

    esC0 = contextlib.ExitStack()
    if xn1T is None:
        xn1Ta = A("c_xn1Ta", [128, KD * T], BF16)
        xn1T = [xn1Ta[:, T * k:T * (k + 1)] for k in range(KD)]
        lnst = {n: esC0.enter_context(nc.sbuf_tensor(
            f"c_ln_{n}", [128, D if n == "sq" else 1], F32, side="right"))
            for n in ("sq", "s1", "s2", "mean", "std", "rstd", "nb", "nsq")}
        xldt = esC0.enter_context(nc.sbuf_tensor("c_xldt", [128, D], F32,
                                                 side="right"))
        xnbt = esC0.enter_context(nc.sbuf_tensor("c_xnt", [128, D], F32,
                                                 side="right"))
        xnbbt = esC0.enter_context(nc.sbuf_tensor("c_xnbt", [128, D], BF16,
                                                  side="right"))
        for gt in range(NT):
            p.add("sync", lambda e, gt=gt: e.dma_start(
                out=xldt[:], in_=x1[128 * gt:128 * (gt + 1), :]), dma=True,
                writes=["c_xldt"])
            emit_ln(p, nc, lnst, "c_xldt", xldt[:], xnbt[:], ["c_xnt"])
            p.add("scalar", lambda e: e.activation(xnbbt[:], xnbt[:],
                  AF.Copy), reads=["c_xnt"], writes=["c_xnbt"])
            p.add("sync", lambda e, gt=gt: e.dma_start_transpose(
                xn1Ta[:].rearrange("p (c q) -> p c q", q=T)[:, :,
                                   128 * gt:128 * (gt + 1)],
                xnbbt[:].rearrange("p (c q) -> p c q", q=128)), dma=True,
                reads=["c_xnbt"], writes=["c_xn1T"])

    rwb = A("c_rwb", [128, KD * NE], BF16)
    for k in range(KD):
        p.add("gpsimd", lambda e, k=k: e.dma_start(
            out=rwb[:, NE * k:NE * (k + 1)], in_=rw[128 * k:128 * (k + 1), :]),
            dma=True, writes=["c_rwb"])
    rbrow = A("c_rbrow", [1, NE], F32)
    rbb = A("c_rbb", [1, NE], BF16)
    p.add("sync", lambda e: e.dma_start(
        out=rbrow[:], in_=rb.rearrange("(o d) -> o d", o=1)), dma=True,
        writes=["c_rbrow"])
    p.add("scalar", lambda e: e.activation(rbb[:], rbrow[:], AF.Copy),
          reads=["c_rbrow"], writes=["c_rbb"])

    es_psC0 = contextlib.ExitStack()
    lg_ps = es_psC0.enter_context(nc.psum_tensor("c_lgps", [128, NE * NT],
                                                 F32))
    gt_ps = es_psC0.enter_context(nc.psum_tensor("c_gtps", [NE, 128], F32))
    lgall = A("c_lgall", [128, NE * NT], F32)
    t1a = A("c_t1a", [128, NE * NT], F32)
    t2a = A("c_t2a", [128, NE * NT], F32)
    v1 = A("c_v1", [128, NT], F32)
    v2 = A("c_v2", [128, NT], F32)
    gT = A("c_gT", [NE, 128], F32)
    grow = A("c_grow", [1, 512], F32)
    gTd = dram["gTd"]
    bc = lambda t: bass.AP(t, 0, [[NT, 128], [1, NT], [0, NE]])
    b3 = lambda t: t[:].rearrange("p (g e) -> p g e", e=NE)
    for gt in range(NT):
        for k in range(KD):
            p.add("tensor", lambda e, k=k, gt=gt: e.matmul(
                lg_ps[:, NE * gt:NE * (gt + 1)],
                xn1T[k][:, 128 * gt:128 * (gt + 1)],
                rwb[:, NE * k:NE * (k + 1)], start=(k == 0), stop=False),
                reads=["c_xn1T", "c_rwb"], writes=["c_lgps"])
        p.add("tensor", lambda e, gt=gt: e.matmul(
            lg_ps[:, NE * gt:NE * (gt + 1)], ones1b[:], rbb[:],
            start=False, stop=True),
            reads=["c_ones1b", "c_rbb"], writes=["c_lgps"])
    p.add("scalar", lambda e: e.copy(lgall[:], lg_ps[:]),
          reads=["c_lgps"], writes=["c_lgall"])
    p.add("vector", lambda e: e.reduce_max(v1[:], b3(lgall), axis=AX.X),
          reads=["c_lgall"], writes=["c_v1"])
    p.add("vector", lambda e: e.tensor_scalar(
        v1[:], v1[:], -1.0, None, ALU.mult), reads=["c_v1"], writes=["c_v1"])
    p.add("vector", lambda e: e.tensor_tensor(
        t1a[:].rearrange("p (g e) -> p g e", e=NE),
        lgall[:].rearrange("p (g e) -> p g e", e=NE), bc(v1),
        ALU.add), reads=["c_lgall", "c_v1"], writes=["c_t1a"])
    p.add("vector", lambda e: e.tensor_scalar(
        t2a[:], t1a[:], 0.0, None, ALU.is_ge), reads=["c_t1a"],
        writes=["c_t2a"])
    p.add("vector", lambda e: e.scalar_tensor_tensor(
        t2a[:], t2a[:], -1e30, lgall[:], ALU.mult, ALU.add),
        reads=["c_t2a", "c_lgall"], writes=["c_t2a"])
    p.add("vector", lambda e: e.reduce_max(v2[:], b3(t2a), axis=AX.X),
          reads=["c_t2a"], writes=["c_v2"])
    p.add("vector", lambda e: e.tensor_scalar(
        v2[:], v2[:], -1.0, None, ALU.mult), reads=["c_v2"], writes=["c_v2"])
    p.add("vector", lambda e: e.tensor_tensor(
        t2a[:].rearrange("p (g e) -> p g e", e=NE),
        lgall[:].rearrange("p (g e) -> p g e", e=NE), bc(v2),
        ALU.add), reads=["c_lgall", "c_v2"], writes=["c_t2a"])
    p.add("vector", lambda e: e.tensor_scalar(
        t2a[:], t2a[:], 0.0, None, ALU.is_ge), reads=["c_t2a"],
        writes=["c_t2a"])
    p.add("scalar", lambda e: e.activation(t1a[:], t1a[:], AF.Exp),
          reads=["c_t1a"], writes=["c_t1a"])
    p.add("vector", lambda e: e.tensor_tensor(
        t1a[:], t1a[:], t2a[:], ALU.mult), reads=["c_t1a", "c_t2a"],
        writes=["c_t1a"])
    p.add("vector", lambda e: e.reduce_sum(v1[:], b3(t1a), axis=AX.X),
          reads=["c_t1a"], writes=["c_v1"])
    p.add("vector", lambda e: e.reciprocal(v1[:], v1[:]), reads=["c_v1"],
          writes=["c_v1"])
    p.add("vector", lambda e: e.tensor_tensor(
        t1a[:].rearrange("p (g e) -> p g e", e=NE),
        t1a[:].rearrange("p (g e) -> p g e", e=NE), bc(v1),
        ALU.mult), reads=["c_t1a", "c_v1"], writes=["c_t1a"])
    for gt in range(NT):
        p.add("tensor", lambda e, gt=gt: e.transpose(
            gt_ps[:], t1a[:, NE * gt:NE * (gt + 1)], idf[:]),
            reads=["c_t1a", "identf"], writes=["c_gtps"])
        p.add("scalar", lambda e: e.copy(gT[:], gt_ps[:]),
              reads=["c_gtps"], writes=["c_gT"])
        p.add("scalar", lambda e, gt=gt: e.dma_start(
            out=gTd[:, 128 * gt:128 * (gt + 1)], in_=gT[:]), dma=True,
            reads=["c_gT"], writes=["gTd"])
        p.barrier()
    esC0.close()
    es_psC0.close()

    acc = A("c_acc", [128, 16 * D], F32)
    w1b = A("c_w1b", [128, KD * HE], BF16)
    w3b = A("c_w3b", [128, KD * HE], BF16)
    w2b = [A(f"c_w2b{j}", [128, 4 * D], BF16) for j in range(2)]
    sil = A("c_sil", [128, 2 * 512], F32)
    tg = A("c_tg", [128, 512], F32)
    gh1 = A("c_gh1", [128, 4 * 2048], BF16)
    gh2 = A("c_gh2", [128, 4 * 512], BF16)
    hp_ps = P("c_hp", [128, 4 * 512], F32)
    gb_ps = P("c_gb", [128, 512], F32)
    op_ps = P("c_op", [128, D], F32)
    xl2 = A("c_xl2", [128, D], F32)
    sl2 = A("c_sl2", [128, D], F32)
    g2 = A("c_g2", [128, D], F32)

    def wsrc(e):
        if e < NE:
            return (dram["routed_w1"][e], dram["routed_w3"][e],
                    dram["routed_w2"][e], True)
        ee = e - NE
        return (dram["shared_w1"][ee], dram["shared_w3"][ee],
                dram["shared_w2"][ee], False)

    def h_phase(e, half, sp, ghdst, ghkey, ghstride, ghoff):
        _, _, _, routed = wsrc(e)
        t0 = 2048 * half + 512 * sp
        if routed:
            p.add("sync", lambda en, e=e, t0=t0: en.dma_start(
                out=grow[:], in_=gTd[e, t0:t0 + 512].rearrange(
                    "(o t) -> o t", o=1)), dma=True,
                reads=["gTd"], writes=["c_grow"])
            p.add("tensor", lambda en: en.matmul(
                gb_ps[:], onesf[:, 0:128], grow[:],
                start=True, stop=True),
                reads=["c_onesf", "c_grow"], writes=["c_gb"])
        for grp in range(2):
            for k in range(KD):
                for li in range(2):
                    it = 2 * grp + li
                    p.add("tensor", lambda en, it=it, li=li, k=k, t0=t0:
                          en.matmul(
                        hp_ps[:, 512 * li:512 * (li + 1)],
                        w1b[:, HE * k + 128 * it:HE * k + 128 * (it + 1)],
                        xn1T[k][:, t0:t0 + 512],
                        start=(k == 0), stop=(k == KD - 1)),
                        reads=[f"c_w1b{k}", "c_xn1T"], writes=["c_hp"])
            p.add("scalar", lambda en, grp=grp: en.activation(
                sil[:], hp_ps[:, 0:1024], AF.Silu),
                reads=["c_hp"], writes=["c_sil"])
            for k in range(KD):
                for li in range(2):
                    it = 2 * grp + li
                    p.add("tensor", lambda en, it=it, li=li, k=k, t0=t0:
                          en.matmul(
                        hp_ps[:, 1024 + 512 * li:1024 + 512 * (li + 1)],
                        w3b[:, HE * k + 128 * it:HE * k + 128 * (it + 1)],
                        xn1T[k][:, t0:t0 + 512],
                        start=(k == 0), stop=(k == KD - 1)),
                        reads=[f"c_w3b{k}", "c_xn1T"], writes=["c_hp3"])
            for li in range(2):
                it = 2 * grp + li
                p.add("vector", lambda en, li=li: en.tensor_tensor(
                    tg[:], sil[:, 512 * li:512 * (li + 1)],
                    hp_ps[:, 1024 + 512 * li:1024 + 512 * (li + 1)],
                    ALU.mult),
                    reads=["c_sil", "c_hp3"], writes=["c_tg"])
                lo = ghstride * it + ghoff
                if routed:
                    p.add("vector", lambda en, lo=lo: en.tensor_tensor(
                        ghdst[:, lo:lo + 512], tg[:], gb_ps[:], ALU.mult),
                        reads=["c_tg", "c_gb"], writes=[ghkey])
                else:
                    p.add("vector", lambda en, lo=lo: en.tensor_scalar(
                        ghdst[:, lo:lo + 512], tg[:], 1.0, None, ALU.mult),
                        reads=["c_tg"], writes=[ghkey])

    for half in range(2):
        for pr in range(5):
            e1, e2_ = 2 * pr, 2 * pr + 1
            for j, e in enumerate((e1, e2_)):
                _, _, we2, _ = wsrc(e)
                p.add("gpsimd", lambda en, we2=we2, j=j: en.dma_start(
                    out=w2b[j][:].rearrange("p (k d) -> p k d", d=D),
                    in_=we2.rearrange("(k p) d -> p k d", p=128)), dma=True,
                    writes=[w2b[j]])
            we1, we3, _, _ = wsrc(e1)
            for k in range(KD):
                p.add("gpsimd", lambda en, we1=we1, k=k: en.dma_start(
                    out=w1b[:, HE * k:HE * (k + 1)],
                    in_=we1[128 * k:128 * (k + 1), :]), dma=True,
                    writes=[f"c_w1b{k}"])
                p.add("gpsimd", lambda en, we3=we3, k=k: en.dma_start(
                    out=w3b[:, HE * k:HE * (k + 1)],
                    in_=we3[128 * k:128 * (k + 1), :]), dma=True,
                    writes=[f"c_w3b{k}"])
            for sp in range(4):
                h_phase(e1, half, sp, gh1, "c_gh1", 2048, 512 * sp)
            we1, we3, _, _ = wsrc(e2_)
            for k in range(KD):
                p.add("gpsimd", lambda en, we1=we1, k=k: en.dma_start(
                    out=w1b[:, HE * k:HE * (k + 1)],
                    in_=we1[128 * k:128 * (k + 1), :]), dma=True,
                    writes=[f"c_w1b{k}"])
                p.add("gpsimd", lambda en, we3=we3, k=k: en.dma_start(
                    out=w3b[:, HE * k:HE * (k + 1)],
                    in_=we3[128 * k:128 * (k + 1), :]), dma=True,
                    writes=[f"c_w3b{k}"])
            for sp in range(4):
                h_phase(e2_, half, sp, gh2, "c_gh2", 512, 0)
                for tt in range(4):
                    for j in range(2):
                        gh = gh1 if j == 0 else gh2
                        ghkey = "c_gh1" if j == 0 else "c_gh2"
                        stride = 2048 if j == 0 else 512
                        off = 512 * sp if j == 0 else 0
                        for ki in range(4):
                            for c in range(2):
                                st = (j == 0 and ki == 0)
                                sto = (j == 1 and ki == 3)
                                lo = stride * ki + off + 128 * tt
                                p.add("tensor", lambda en, gh=gh, lo=lo,
                                      ki=ki, c=c, st=st, sto=sto, j=j:
                                      en.matmul(
                                    op_ps[:, 512 * c:512 * (c + 1)],
                                    gh[:, lo:lo + 128],
                                    w2b[j][:, D * ki + 512 * c:
                                           D * ki + 512 * (c + 1)],
                                    start=st, stop=sto),
                                    reads=[ghkey, w2b[j]], writes=["c_op"])
                    blk = 1024 * (4 * sp + tt)
                    if pr == 0:
                        p.add("scalar", lambda en, blk=blk: en.copy(
                            acc[:, blk:blk + D], op_ps[:]),
                            reads=["c_op"], writes=["c_acc"])
                    else:
                        p.add("vector", lambda en, blk=blk: en.tensor_tensor(
                            acc[:, blk:blk + D], acc[:, blk:blk + D],
                            op_ps[:], ALU.add), reads=["c_acc", "c_op"],
                            writes=["c_acc"])
        for tt16 in range(16):
            gt = 16 * half + tt16
            blk = 1024 * tt16
            p.add("scalar", lambda en, blk=blk: en.activation(
                g2[:], acc[:, blk:blk + D], AF.Silu),
                reads=["c_acc"], writes=["c_g2"])
            p.add("sync", lambda en, gt=gt: en.dma_start(
                out=xl2[:], in_=x1[128 * gt:128 * (gt + 1), :]), dma=True,
                writes=["c_xl2"])
            p.add("sync", lambda en, gt=gt: en.dma_start(
                out=sl2[:], in_=s2[128 * gt:128 * (gt + 1), :]), dma=True,
                writes=["c_sl2"])
            p.add("vector", lambda en: en.tensor_tensor(
                sl2[:], sl2[:], g2[:], ALU.mult),
                reads=["c_sl2", "c_g2"], writes=["c_sl2"])
            p.add("vector", lambda en: en.tensor_tensor(
                sl2[:], sl2[:], xl2[:], ALU.add),
                reads=["c_sl2", "c_xl2"], writes=["c_sl2"])
            p.add("scalar", lambda en, gt=gt: en.dma_start(
                out=y[128 * gt:128 * (gt + 1), :], in_=sl2[:]), dma=True,
                reads=["c_sl2"], writes=["y"])
    p.barrier()
    es.close()
    return p


_CONST_UC = np.zeros((128, 2), dtype=np.uint32)
_CONST_UC[:, 0] = 0x80000000
_CONST_UC[:, 1] = 0xFFC00000


def _common_consts(nc):
    import ml_dtypes
    d = {}
    d["uconst"] = nc.inline_tensor(_CONST_UC, "c_uconst").ap()
    d["identb"] = nc.inline_tensor(
        np.eye(128, dtype=ml_dtypes.bfloat16), "c_identb").ap()
    d["identf"] = nc.inline_tensor(
        np.eye(128, dtype=np.float32), "c_identf").ap()
    tri = np.where(np.arange(128)[None, :] < np.arange(128)[:, None],
                   np.float32(-1e9), np.float32(0.0)).astype(np.float32)
    d["trimask"] = nc.inline_tensor(tri, "c_trimask").ap()
    return d


def build_F_program(serialize=False):
    nc = bass.Bass("TRN2", target_bir_lowering=False)
    dram = _common_consts(nc)
    dram["x0"] = nc.dram_tensor("x0", [T, D], F32, kind="ExternalInput").ap()
    dram["wg"] = nc.dram_tensor("wg", [I_FFN, D], F32, kind="ExternalInput").ap()
    dram["wu"] = nc.dram_tensor("wu", [I_FFN, D], F32, kind="ExternalInput").ap()
    dram["wd"] = nc.dram_tensor("wd", [D, I_FFN], F32, kind="ExternalInput").ap()
    dram["g0"] = nc.dram_tensor("g0", [T, D], F32, kind="ExternalOutput").ap()
    dram["wdT"] = nc.dram_tensor("wdT", [KI, 128, D], BF16, kind="Internal").ap()
    p = Prog(nc, serialize=serialize)
    build_phaseF(nc, p, dram)
    p.emit()
    return nc


def build_A_program(serialize=False):
    nc = bass.Bass("TRN2", target_bir_lowering=False)
    dram = _common_consts(nc)
    dram["x0"] = nc.dram_tensor("x0", [T, D], F32, kind="ExternalInput").ap()
    dram["g0"] = nc.dram_tensor("g0", [T, D], F32, kind="ExternalInput").ap()
    dram["w_in"] = nc.dram_tensor("w_in", [D, 3 * D], F32,
                                  kind="ExternalInput").ap()
    dram["wnorm"] = nc.dram_tensor("wnorm", [D], F32,
                                   kind="ExternalInput").ap()
    dram["lam"] = nc.dram_tensor("lam", [D], F32, kind="ExternalInput").ap()
    dram["wout"] = nc.dram_tensor("wout", [D, D], F32,
                                  kind="ExternalInput").ap()
    dram["x1"] = nc.dram_tensor("x1", [T, D], F32, kind="ExternalOutput").ap()
    p = Prog(nc, serialize=serialize)
    build_phaseA(nc, p, dram)
    p.emit()
    return nc


def build_B_program(serialize=False):
    nc = bass.Bass("TRN2", target_bir_lowering=False)
    dram = _common_consts(nc)
    dram["x1"] = nc.dram_tensor("x1", [T, D], F32, kind="ExternalInput").ap()
    dram["wqkv"] = nc.dram_tensor("wqkv", [D, 3 * D], F32,
                                  kind="ExternalInput").ap()
    dram["bqkv"] = nc.dram_tensor("bqkv", [3 * D], F32,
                                  kind="ExternalInput").ap()
    dram["woutB"] = nc.dram_tensor("woutB", [D, D], F32,
                                   kind="ExternalInput").ap()
    dram["boutB"] = nc.dram_tensor("boutB", [D], F32,
                                   kind="ExternalInput").ap()
    dram["s2"] = nc.dram_tensor("s2", [T, D], F32, kind="ExternalOutput").ap()
    dram["vAll"] = nc.dram_tensor("vAll", [T, D], BF16, kind="Internal").ap()
    p = Prog(nc, serialize=serialize)
    build_phaseB(nc, p, dram)
    p.emit()
    return nc


def _c_dram(nc, dram):
    dram["x1"] = nc.dram_tensor("x1", [T, D], F32, kind="ExternalInput").ap()
    dram["s2"] = nc.dram_tensor("s2", [T, D], F32, kind="ExternalInput").ap()
    dram["router_w"] = nc.dram_tensor("router_w", [D, NE], F32,
                                      kind="ExternalInput").ap()
    dram["router_b"] = nc.dram_tensor("router_b", [NE], F32,
                                      kind="ExternalInput").ap()
    dram["routed_w1"] = nc.dram_tensor("routed_w1", [NE, D, HE], F32,
                                       kind="ExternalInput").ap()
    dram["routed_w3"] = nc.dram_tensor("routed_w3", [NE, D, HE], F32,
                                       kind="ExternalInput").ap()
    dram["routed_w2"] = nc.dram_tensor("routed_w2", [NE, HE, D], F32,
                                       kind="ExternalInput").ap()
    dram["shared_w1"] = nc.dram_tensor("shared_w1", [2, D, HE], F32,
                                       kind="ExternalInput").ap()
    dram["shared_w3"] = nc.dram_tensor("shared_w3", [2, D, HE], F32,
                                       kind="ExternalInput").ap()
    dram["shared_w2"] = nc.dram_tensor("shared_w2", [2, HE, D], F32,
                                       kind="ExternalInput").ap()
    dram["y"] = nc.dram_tensor("y", [T, D], F32, kind="ExternalOutput").ap()


def build_C_program(serialize=False):
    nc = bass.Bass("TRN2", target_bir_lowering=False)
    dram = _common_consts(nc)
    _c_dram(nc, dram)
    dram["gTd"] = nc.dram_tensor("gTd", [NE, T], F32, kind="Internal").ap()
    p = Prog(nc, serialize=serialize)
    build_phaseC(nc, p, dram)
    p.emit()
    return nc


def build_full_program():
    nc = bass.Bass("TRN2", target_bir_lowering=False)
    dram = _common_consts(nc)
    EI = lambda n, sh: nc.dram_tensor(n, sh, F32, kind="ExternalInput").ap()
    dram["x0"] = EI("x0", [T, D])
    dram["wg"] = EI("wg", [I_FFN, D])
    dram["wu"] = EI("wu", [I_FFN, D])
    dram["wd"] = EI("wd", [D, I_FFN])
    dram["w_in"] = EI("w_in", [D, 3 * D])
    dram["wnorm"] = EI("wnorm", [D])
    dram["lam"] = EI("lam", [D])
    dram["wout"] = EI("wout", [D, D])
    dram["wqkv"] = EI("wqkv", [D, 3 * D])
    dram["bqkv"] = EI("bqkv", [3 * D])
    dram["woutB"] = EI("woutB", [D, D])
    dram["boutB"] = EI("boutB", [D])
    dram["router_w"] = EI("router_w", [D, NE])
    dram["router_b"] = EI("router_b", [NE])
    dram["routed_w1"] = EI("routed_w1", [NE, D, HE])
    dram["routed_w3"] = EI("routed_w3", [NE, D, HE])
    dram["routed_w2"] = EI("routed_w2", [NE, HE, D])
    dram["shared_w1"] = EI("shared_w1", [2, D, HE])
    dram["shared_w3"] = EI("shared_w3", [2, D, HE])
    dram["shared_w2"] = EI("shared_w2", [2, HE, D])
    IN = lambda n, sh, dt: nc.dram_tensor(n, sh, dt, kind="Internal").ap()
    dram["g0"] = IN("g0", [T, D], F32)
    dram["x1"] = IN("x1", [T, D], F32)
    dram["s2"] = IN("s2", [T, D], F32)
    dram["vAll"] = IN("vAll", [T, D], BF16)
    dram["wdT"] = IN("wdT", [KI, 128, D], BF16)
    dram["gTd"] = IN("gTd", [NE, T], F32)
    dram["y"] = nc.dram_tensor("y", [T, D], F32, kind="ExternalOutput").ap()
    p = Prog(nc)
    ln_cache = (nc.alloc_sbuf_tensor("ln0_rstd", [128, NT], F32),
                nc.alloc_sbuf_tensor("ln0_nb", [128, NT], F32))
    build_phaseF(nc, p, dram, ln_cache=ln_cache)
    build_phaseA(nc, p, dram, ln_cache=ln_cache)
    build_phaseB(nc, p, dram)
    build_phaseC(nc, p, dram)
    p.emit()
    return nc


_PROGS = {}


def _get_progs():
    if not _PROGS:
        _PROGS["full"] = build_full_program()
    return _PROGS


def kernel(**inputs):
    progs = _get_progs()
    x = np.ascontiguousarray(np.asarray(inputs["L0_attn_x"], dtype=np.float32))
    asf = lambda k: np.ascontiguousarray(np.asarray(inputs[k],
                                                    dtype=np.float32))
    shared = {
        "wg": asf("L0_ffn_w_gate"), "wu": asf("L0_ffn_w_up"),
        "wd": asf("L0_ffn_w_down"), "w_in": asf("L0_attn_W_in"),
        "wnorm": asf("L0_attn_w_norm"), "lam": asf("L0_attn_lambda_log"),
        "wout": asf("L0_attn_W_out"), "wqkv": asf("L1_attn_w_qkv"),
        "bqkv": asf("L1_attn_b_qkv"), "woutB": asf("L1_attn_w_out"),
        "boutB": asf("L1_attn_b_out"), "router_w": asf("L1_ffn_router_w"),
        "router_b": asf("L1_ffn_router_b"),
        "routed_w1": asf("L1_ffn_routed_w1"),
        "routed_w3": asf("L1_ffn_routed_w3"),
        "routed_w2": asf("L1_ffn_routed_w2"),
        "shared_w1": asf("L1_ffn_shared_w1"),
        "shared_w3": asf("L1_ffn_shared_w3"),
        "shared_w2": asf("L1_ffn_shared_w2"),
    }
    cores = list(range(8))
    res = run_bass_kernel_spmd(progs["full"], [
        dict(shared, x0=x[i]) for i in cores], core_ids=cores)
    out = np.stack([np.asarray(res.results[i]["y"], dtype=np.float32)
                    for i in cores], axis=0)
    return out
